# version 12
# baseline (speedup 1.0000x reference)
import numpy as np
from contextlib import ExitStack

import concourse.bass as bass
import concourse.mybir as mybir
from concourse.bass_utils import run_bass_kernel_spmd

F32 = mybir.dt.float32
BF16 = mybir.dt.bfloat16
AF = mybir.ActivationFunctionType
ALU = mybir.AluOpType
AX = mybir.AxisListType.X

D = 1024
S = 2048
NB = 2
NQ = 4
TOKQ = 1024
NE = 8
NI = 2048
NST = 16
EPS_RMS = 1e-6
EPS_LN = 1e-5
EPS_RT = 1e-6


class Planner:
    ENGINES = ("sync", "gpsimd", "tensor", "scalar", "vector")

    def __init__(self):
        self.cmds = {e: [] for e in self.ENGINES}
        self.val = {}

    def emit(self, eng, fn, wait=(), inc=None):
        newv = None
        if inc is not None:
            sname, by = inc
            newv = self.val.get(sname, 0) + by
            self.val[sname] = newv
        self.cmds[eng].append((tuple(wait), fn, inc, newv))
        return newv

    def cur(self, sname):
        return self.val.get(sname, 0)

    def replay(self, eng_name, eng, sems):
        for wait, fn, inc, newv in self.cmds[eng_name]:
            for sname, v in wait:
                eng.wait_ge(sems[sname], v)
            ins = fn(eng)
            if inc is not None:
                assert ins is not None, f"inc {inc} on None instruction"
                ins.then_inc(sems[inc[0]], inc[1])


class _Stop(Exception):
    pass


def build_nc(nq=NQ, ne=NE, dbg=None):
    nc = bass.Bass("TRN2", target_bir_lowering=False)

    X = nc.dram_tensor("x_fm", [NB, D, S], F32, kind="ExternalInput").ap()
    WLB = nc.dram_tensor("wlamb", [D, 2 * NST], F32, kind="ExternalInput").ap()
    BLAM = nc.dram_tensor("blam", [NST, 1], F32, kind="ExternalInput").ap()
    WCT = nc.dram_tensor("wct", [NST, D], F32, kind="ExternalInput").ap()
    WGT = nc.dram_tensor("wgt", [D, D], F32, kind="ExternalInput").ap()
    RMSW = nc.dram_tensor("rmsw", [128, 8], F32, kind="ExternalInput").ap()
    RWT = nc.dram_tensor("rwt", [D, NE], F32, kind="ExternalInput").ap()
    W13R = nc.dram_tensor("w13r", [NE, 256, 128, 128], F32, kind="ExternalInput").ap()
    W2R = nc.dram_tensor("w2r", [NE, 128, 128, 128], F32, kind="ExternalInput").ap()
    OHE = nc.dram_tensor("ohe", [NE, 128 * NE], F32, kind="ExternalInput").ap()
    IDENT = nc.dram_tensor("ident", [128, 128], F32, kind="ExternalInput").ap()
    OUT = nc.dram_tensor("out_fm", [NB, D, S], F32, kind="ExternalOutput").ap()
    W13B = nc.dram_tensor("w13b_scr", [NE, 16, 128, 2048], BF16, kind="Internal").ap()
    W2B = nc.dram_tensor("w2b_scr", [NE, 8, 128, 2048], BF16, kind="Internal").ap()
    DBG = None
    if dbg is not None:
        DBG = nc.dram_tensor("dbg", [128, 8192], F32, kind="ExternalOutput").ap()

    p = Planner()

    with ExitStack() as st:
        sb = lambda n, s, d: st.enter_context(nc.sbuf_tensor(n, s, d))
        psa = lambda n, s, d: st.enter_context(nc.psum_tensor(n, s, d))

        A = sb("A", [128, 8192], F32)
        C = sb("C", [128, 8192], F32)
        H = sb("H", [128, 16384], BF16)
        XBF = sb("XBF", [128, 8192], BF16)
        LAMS = sb("LAMS", [NST, TOKQ], F32)
        BXS = sb("BXS", [NST, TOKQ], F32)
        HS0 = sb("HS0", [NST, TOKQ], F32)
        HS1 = sb("HS1", [NST, TOKQ], F32)
        HSBF = sb("HSBF", [NST, TOKQ], BF16)
        WGTS = sb("WGTS", [128, 8192], BF16)
        WLBS = sb("WLBS", [128, 8 * 32], BF16)
        WCTS = sb("WCTS", [NST, D], BF16)
        RWS = sb("RWS", [128, 8 * NE], BF16)
        RMSWS = sb("RMSWS", [128, 8], F32)
        BLAMS = sb("BLAMS", [NST, 1], F32)
        OHES = sb("OHES", [NE, 128 * NE], BF16)
        IDS = sb("IDS", [128, 128], F32)
        ONESB = sb("ONESB", [128, 1], BF16)
        ONES1 = sb("ONES1", [1, 128], F32)
        WR13 = sb("WR13", [128, 2 * 2048], BF16)
        WR2 = sb("WR2", [128, 2 * 2048], BF16)
        GTS = sb("GTS", [128, 1024], BF16)
        SILU = sb("SILU", [128, 1024], BF16)
        SQB = sb("SQB", [128, 2048], BF16)
        RBF = sb("RBF", [128, 1024], BF16)
        TSC = sb("TSC", [128, 1024], F32)
        CEB = sb("CEB", [128, 1024], F32)
        ROWA = sb("ROWA", [1, 1024], F32)
        ROWB = sb("ROWB", [1, 1024], F32)
        ROWC = sb("ROWC", [1, 1024], F32)
        ROWD = sb("ROWD", [1, 1024], F32)
        LOG = sb("LOG", [128, 64], F32)
        PRB = sb("PRB", [128, 64], F32)
        MSK1 = sb("MSK1", [128, 64], F32)
        MSK2 = sb("MSK2", [128, 64], F32)
        NMS = sb("NMS", [128, 64], F32)
        PM = sb("PM", [128, 64], F32)
        CES = sb("CES", [128, 64], F32)
        CEF = sb("CEF", [NE, TOKQ], BF16)
        ST = sb("ST", [128, 96], F32)

        P = [psa(f"P{i}", [128, 512], F32) for i in range(8)]

        sems = {}
        for sname in ("s_xp", "s_xq", "s_xb", "s_wp", "w13s0", "w13s1",
                      "w2s0", "w2s1", "w13w0", "w13w1", "w2w0", "w2w1",
                      "s_o", "t_pe", "a_act", "v_dve", "g_pl"):
            sems[sname] = st.enter_context(nc.semaphore(sname))

        p.emit("sync", lambda e: e.dma_start(out=RMSWS[:], in_=RMSW), inc=("s_xp", 16))
        p.emit("sync", lambda e: e.dma_start(out=BLAMS[:], in_=BLAM), inc=("s_xp", 16))
        p.emit("sync", lambda e: e.dma_start(out=IDS[:], in_=IDENT), inc=("s_xp", 16))
        p.emit("gpsimd", lambda e: e.dma_start(
            out=WLBS[:].rearrange("p (k n) -> p k n", k=8),
            in_=WLB.rearrange("(k p) n -> p k n", p=128)), inc=("s_wp", 16))
        p.emit("gpsimd", lambda e: e.dma_start(out=WCTS[:], in_=WCT), inc=("s_wp", 16))
        p.emit("gpsimd", lambda e: e.dma_start(
            out=WGTS[:].rearrange("p (k n) -> p k n", k=8),
            in_=WGT.rearrange("(k p) n -> p k n", p=128)), inc=("s_wp", 16))
        p.emit("gpsimd", lambda e: e.dma_start(
            out=RWS[:].rearrange("p (k n) -> p k n", k=8),
            in_=RWT.rearrange("(k p) n -> p k n", p=128)), inc=("s_wp", 16))
        p.emit("gpsimd", lambda e: e.dma_start(out=OHES[:], in_=OHE), inc=("s_wp", 16))
        prologue_w = p.cur("s_wp")
        prologue_x = p.cur("s_xp")
        p.emit("vector", lambda e: e.memset(ONESB[:], 1.0))
        p.emit("vector", lambda e: e.memset(ONES1[:], 1.0))
        CST = sb("CST", [128, 4], F32)
        p.emit("vector", lambda e: e.memset(CST[:, 0:1], EPS_RMS))
        p.emit("vector", lambda e: e.memset(CST[:, 1:2], EPS_LN))
        p.emit("vector", lambda e: e.memset(CST[:, 2:3], 1.0 / D))
        v_consts = p.emit("vector", lambda e: e.memset(CST[:, 3:4], -0.5),
                          inc=("v_dve", 1))
        nc.const_aps.register(F32, EPS_RMS, CST[:, 0:1])
        nc.const_aps.register(F32, EPS_LN, CST[:, 1:2])
        nc.const_aps.register(F32, 1.0 / D, CST[:, 2:3])
        nc.const_aps.register(F32, -0.5, CST[:, 3:4])

        w13_consumed = {}
        w2_consumed = {}
        w13_loaded = {}
        w2_loaded = {}
        w13_issued = 0
        w2_issued = 0

        wb_count = {"w13w0": 0, "w13w1": 0, "w2w0": 0, "w2w1": 0}
        w13_wbq = lambda e_: 0 if e_ < 4 else 1
        w13_done = {}
        w2_done = {}

        def issue_w13(q, e_, chunk):
            nonlocal w13_issued
            g = w13_issued
            slot = g % 2
            ssem = "w13s%d" % slot
            wsem = "w13w%d" % slot
            cached = w13_done.get(e_, False)
            do_wb = (not cached) and q == w13_wbq(e_)
            wait = []
            if g >= 2:
                wait.append(("t_pe", w13_consumed[g - 2]))
                if wb_count[wsem]:
                    wait.append((wsem, 16 * wb_count[wsem]))
            if not cached and not do_wb and q > w13_wbq(e_):
                cached = True
            if cached:
                def fn(eng, e_=e_, chunk=chunk, slot=slot):
                    return eng.dma_start(
                        out=WR13[:, slot * 2048:(slot + 1) * 2048],
                        in_=W13B[e_, chunk])
                fill = p.emit("gpsimd", fn, wait=wait, inc=(ssem, 16))
            else:
                def fn(eng, e_=e_, chunk=chunk, slot=slot):
                    return eng.dma_start(
                        out=WR13[:, slot * 2048:(slot + 1) * 2048].rearrange(
                            "p (t f) -> p t f", t=16),
                        in_=W13R[e_, chunk * 16:(chunk + 1) * 16].rearrange(
                            "t p f -> p t f"))
                fill = p.emit("gpsimd", fn, wait=wait, inc=(ssem, 16))
            w13_loaded[g] = (ssem, fill)
            if do_wb:
                def fw(eng, e_=e_, chunk=chunk, slot=slot):
                    return eng.dma_start(
                        out=W13B[e_, chunk],
                        in_=WR13[:, slot * 2048:(slot + 1) * 2048])
                p.emit("gpsimd", fw, wait=[(ssem, fill)], inc=(wsem, 16))
                wb_count[wsem] += 1
                if chunk == 15:
                    w13_done[e_] = True
            w13_issued += 1
            return g

        def issue_w2(q, e_, chunk):
            nonlocal w2_issued
            g = w2_issued
            slot = g % 2
            ssem = "w2s%d" % slot
            wsem = "w2w%d" % slot
            cached = w2_done.get(e_, False)
            do_wb = (not cached) and q == 2
            wait = []
            if g >= 2:
                wait.append(("t_pe", w2_consumed[g - 2]))
                if wb_count[wsem]:
                    wait.append((wsem, 16 * wb_count[wsem]))
            if cached:
                def fn(eng, e_=e_, chunk=chunk, slot=slot):
                    return eng.dma_start(
                        out=WR2[:, slot * 2048:(slot + 1) * 2048],
                        in_=W2B[e_, chunk])
                fill = p.emit("gpsimd", fn, wait=wait, inc=(ssem, 16))
            else:
                def fn(eng, e_=e_, chunk=chunk, slot=slot):
                    return eng.dma_start(
                        out=WR2[:, slot * 2048:(slot + 1) * 2048].rearrange(
                            "p (t f) -> p t f", t=16),
                        in_=W2R[e_, chunk * 16:(chunk + 1) * 16].rearrange(
                            "t p f -> p t f"))
                fill = p.emit("gpsimd", fn, wait=wait, inc=(ssem, 16))
            w2_loaded[g] = (ssem, fill)
            if do_wb:
                def fw(eng, e_=e_, chunk=chunk, slot=slot):
                    return eng.dma_start(
                        out=W2B[e_, chunk],
                        in_=WR2[:, slot * 2048:(slot + 1) * 2048])
                p.emit("gpsimd", fw, wait=[(ssem, fill)], inc=(wsem, 16))
                wb_count[wsem] += 1
                if chunk == 7:
                    w2_done[e_] = True
            w2_issued += 1
            return g

        last_ffn_pe = 0
        out_dma_count = 0
        prev_ceb_a = None
        prev_vh15 = None
        prev_vout15 = None
        prev_vres2_15 = None
        prev_vres1_15 = None

        try:
          for q in range(nq):
            b = q // 2
            s0 = (q % 2) * TOKQ
            HSq = HS0 if q % 2 == 0 else HS1
            HSprev = HS1 if q % 2 == 0 else HS0

            xw = [("s_xq", 16 * q)]
            if prev_vres1_15 is not None:
                xw.append(("v_dve", prev_vres1_15))
            xq_val = p.emit("scalar", lambda e, b=b, s0=s0: e.dma_start(
                out=A[:].rearrange("p (db s) -> p db s", db=8),
                in_=X[b, :, s0:s0 + TOKQ].rearrange("(db p) s -> p db s", p=128)),
                wait=xw, inc=("s_xq", 16))

            w = [("s_xb", 16 * q)]
            if last_ffn_pe:
                w.append(("t_pe", last_ffn_pe))
            if prev_vres2_15 is not None:
                w.append(("v_dve", prev_vres2_15))
            xbf_val = p.emit("gpsimd", lambda e, b=b, s0=s0: e.dma_start(
                out=XBF[:].rearrange("p (db s) -> p db s", db=8),
                in_=X[b, :, s0:s0 + TOKQ].rearrange("(db p) s -> p db s", p=128)),
                wait=w, inc=("s_xb", 16))

            wlam = [("s_xb", xbf_val), ("s_wp", prologue_w)]
            if prev_ceb_a is not None:
                wlam += [("a_act", prev_ceb_a), ("v_dve", prev_vh15)]
            for g in range(2):
                for k in range(8):
                    def fn(eng, g=g, k=k):
                        return eng.matmul(
                            P[4 + g][0:NST, :],
                            WLBS[:, k * 32:k * 32 + NST],
                            XBF[:, k * 1024 + g * 512:k * 1024 + g * 512 + 512],
                            start=(k == 0), stop=(k == 7))
                    p.emit("tensor", fn, wait=wlam if (g == 0 and k == 0) else ())
            for g in range(2):
                for k in range(8):
                    def fn(eng, g=g, k=k):
                        return eng.matmul(
                            P[6 + g][0:NST, :],
                            WLBS[:, k * 32 + NST:k * 32 + 32],
                            XBF[:, k * 1024 + g * 512:k * 1024 + g * 512 + 512],
                            start=(k == 0), stop=(k == 7))
                    is_last = (g == 1 and k == 7)
                    t_lam = p.emit("tensor", fn,
                                   inc=("t_pe", 1) if is_last else None)

            wv = [("t_pe", t_lam), ("s_xp", prologue_x), ("v_dve", v_consts)]
            for g in range(2):
                p.emit("scalar", lambda e, g=g: e.activation(
                    LAMS[:, g * 512:(g + 1) * 512], P[4 + g][0:NST, :],
                    AF.Sigmoid, bias=BLAMS[:], scale=1.0),
                    wait=wv if g == 0 else ())
            for g in range(2):
                a_lam = p.emit("scalar", lambda e, g=g: e.copy(
                    BXS[:, g * 512:(g + 1) * 512], P[6 + g][0:NST, :]),
                    inc=("a_act", 1) if g == 1 else None)

            init = 0.0 if q % 2 == 0 else HSprev[:, TOKQ - 1:TOKQ]
            v_scan = p.emit("vector", lambda e, init=init, HSq=HSq: e.tensor_tensor_scan(
                HSq[:], LAMS[:], BXS[:], init, ALU.mult, ALU.add),
                wait=[("a_act", a_lam)], inc=("v_dve", 1))
            a_hsbf = p.emit("scalar", lambda e, HSq=HSq: e.copy(HSBF[:], HSq[:]),
                            wait=[("v_dve", v_scan)], inc=("a_act", 1))
            if dbg == "hs" and q == 0:
                p.emit("sync", lambda e, HSq=HSq: e.dma_start(
                    out=DBG[0:NST, 0:TOKQ], in_=HSq[:]),
                    wait=[("v_dve", v_scan)], inc=("s_o", 16))
                out_dma_count += 1
                raise _Stop

            t_gy = [0] * 16
            a_g = [0] * 16
            v_yg = [0] * 16
            a_sq = [0] * 16
            t_rs_i = [0] * 16

            def emit_rsum(j):
                db, g = j // 2, j % 2
                wr = [("a_act", a_sq[j])]
                if j == 0 and prev_vout15 is not None:
                    wr.append(("v_dve", prev_vout15))
                def fnr(eng, db=db, g=g, j=j):
                    return eng.matmul(
                        P[g][0:1, :], ONESB[:],
                        SQB[:, (j % 4) * 512:(j % 4) * 512 + 512],
                        start=(db == 0), stop=(db == 7))
                t_rs_i[j] = p.emit("tensor", fnr, wait=wr,
                                   inc=("t_pe", 1))

            for i in range(16):
                db, g = i // 2, i % 2
                pg = P[4 + (i % 2)]
                py = P[6 + (i % 2)]
                w = []
                if i == 0:
                    w.append(("a_act", a_hsbf))
                if i >= 2:
                    w.append(("a_act", a_g[i - 2]))
                    w.append(("v_dve", v_yg[i - 2]))
                for k in range(8):
                    def fn(eng, db=db, g=g, k=k, pg=pg):
                        return eng.matmul(
                            pg[:],
                            WGTS[:, k * 1024 + db * 128:k * 1024 + db * 128 + 128],
                            XBF[:, k * 1024 + g * 512:k * 1024 + g * 512 + 512],
                            start=(k == 0), stop=(k == 7))
                    p.emit("tensor", fn, wait=w if k == 0 else ())
                def fny(eng, db=db, g=g, py=py):
                    return eng.matmul(
                        py[:],
                        WCTS[0:NST, db * 128:(db + 1) * 128],
                        HSBF[0:NST, g * 512:(g + 1) * 512],
                        start=True, stop=True)
                t_gy[i] = p.emit("tensor", fny, inc=("t_pe", 1))
                if i >= 2:
                    emit_rsum(i - 2)
                w = [("t_pe", t_gy[i])]
                if i >= 2:
                    w.append(("v_dve", v_yg[i - 2]))
                a_g[i] = p.emit("scalar", lambda e, g=g, pg=pg: e.activation(
                    GTS[:, g * 512:(g + 1) * 512], pg[:], AF.Sigmoid),
                    wait=w, inc=("a_act", 1))
                wyg = [("a_act", a_g[i]), ("t_pe", t_gy[i])]
                if i == 0 and q > 0:
                    wyg.append(("s_o", p.cur("s_o")))
                v_yg[i] = p.emit("vector", lambda e, db=db, g=g, py=py: e.tensor_mul(
                    C[:, db * 1024 + g * 512:db * 1024 + g * 512 + 512],
                    GTS[:, g * 512:(g + 1) * 512], py[:]),
                    wait=wyg, inc=("v_dve", 1))
                ws = [("v_dve", v_yg[i])]
                if i >= 4:
                    ws.append(("t_pe", t_rs_i[i - 4]))
                a_sq[i] = p.emit("scalar", lambda e, db=db, g=g, i=i: e.activation(
                    SQB[:, (i % 4) * 512:(i % 4) * 512 + 512],
                    C[:, db * 1024 + g * 512:db * 1024 + g * 512 + 512],
                    AF.Square),
                    wait=ws, inc=("a_act", 1))
            emit_rsum(14)
            emit_rsum(15)
            t_rs = t_rs_i[15]

            if dbg == "yg" and q == 0:
                p.emit("sync", lambda e: e.dma_start(out=DBG, in_=C[:]),
                    wait=[("v_dve", v_yg[15])], inc=("s_o", 16))
                out_dma_count += 1
                raise _Stop

            a_rr = None
            for g in range(2):
                p.emit("scalar", lambda e, g=g: e.activation(
                    ROWC[0:1, g * 512:(g + 1) * 512], P[g][0:1, :],
                    AF.Ln, bias=EPS_RMS, scale=1.0 / D),
                    wait=[("t_pe", t_rs)] if g == 0 else ())
            a_rr = p.emit("scalar", lambda e: e.activation(
                ROWD[:], ROWC[:], AF.Exp, scale=-0.5), inc=("a_act", 1))
            for g in range(2):
                wrb = [("a_act", a_rr)]
                if g == 0 and prev_vout15 is not None:
                    wrb.append(("v_dve", prev_vout15))
                t_rb = p.emit("tensor", lambda e, g=g: e.matmul(
                    P[2 + g][:], ONES1[:], ROWD[0:1, g * 512:(g + 1) * 512],
                    start=True, stop=True),
                    wait=wrb if g == 0 else (),
                    inc=("t_pe", 1) if g == 1 else None)

            v_res = [0] * 16
            for i in range(16):
                db, g = i // 2, i % 2
                def fn1(eng, db=db, g=g):
                    return eng.scalar_tensor_tensor(
                        TSC[:, g * 512:(g + 1) * 512],
                        C[:, db * 1024 + g * 512:db * 1024 + g * 512 + 512],
                        RMSWS[:, db:db + 1], P[2 + g][:],
                        ALU.mult, ALU.mult)
                v_t1 = p.emit("vector", fn1,
                              wait=[("t_pe", t_rb)] if i == 0 else (),
                              inc=("v_dve", 1))
                def fn2(eng, db=db, g=g):
                    return eng.tensor_add(
                        C[:, db * 1024 + g * 512:db * 1024 + g * 512 + 512],
                        TSC[:, g * 512:(g + 1) * 512],
                        A[:, db * 1024 + g * 512:db * 1024 + g * 512 + 512])
                v_res[i] = p.emit("vector", fn2, wait=[("v_dve", v_t1)],
                                  inc=("v_dve", 1))

            if dbg == "r1" and q == 0:
                p.emit("sync", lambda e: e.dma_start(out=DBG, in_=C[:]),
                    wait=[("v_dve", v_res[15])], inc=("s_o", 16))
                out_dma_count += 1
                raise _Stop

            def layer_norm(src, dst, v_src, s1p, s2p, mbp, rbp, pe_war_wait):
                t_sum_i = [0] * 16
                t_sum = None
                for i in range(16):
                    db, g = i // 2, i % 2
                    sl = slice(db * 1024 + g * 512, db * 1024 + g * 512 + 512)
                    gl = slice(g * 512, g * 512 + 512)
                    w = [("v_dve", v_src[i])]
                    if i >= 2:
                        w.append(("t_pe", t_sum_i[i - 2]))
                    p.emit("scalar", lambda e, sl=sl, gl=gl: e.copy(
                        RBF[:, gl], src[:, sl]), wait=w)
                    a_cast = p.emit("scalar", lambda e, sl=sl, gl=gl: e.activation(
                        SQB[:, gl], src[:, sl], AF.Square), inc=("a_act", 1))
                    wp = [("a_act", a_cast)]
                    if i == 0:
                        wp += pe_war_wait
                    def fns1(eng, db=db, g=g, gl=gl):
                        return eng.matmul(s1p[g], ONESB[:], RBF[:, gl],
                                          start=(db == 0), stop=(db == 7))
                    p.emit("tensor", fns1, wait=wp)
                    def fns2(eng, db=db, g=g, gl=gl):
                        return eng.matmul(s2p[g], ONESB[:], SQB[:, gl],
                                          start=(db == 0), stop=(db == 7))
                    t_sum_i[i] = p.emit("tensor", fns2, inc=("t_pe", 1))
                t_sum = t_sum_i[15]
                a_m = p.emit("scalar", lambda e: e.activation(
                    ROWA[0:1, 0:512], s1p[0], AF.Copy, scale=1.0 / D),
                    wait=[("t_pe", t_sum)])
                a_m = p.emit("scalar", lambda e: e.activation(
                    ROWA[0:1, 512:1024], s1p[1], AF.Copy, scale=1.0 / D),
                    inc=("a_act", 1))
                a_m2 = p.emit("scalar", lambda e: e.activation(
                    ROWB[:], ROWA[:], AF.Square), wait=[("a_act", a_m)],
                    inc=("a_act", 1))
                v_var = None
                for g in range(2):
                    gl = slice(g * 512, g * 512 + 512)
                    def fnv(eng, g=g, gl=gl):
                        return eng.scalar_tensor_tensor(
                            ROWC[0:1, gl], s2p[g], 1.0 / D, ROWB[0:1, gl],
                            ALU.mult, ALU.subtract)
                    v_var = p.emit("vector", fnv,
                                   wait=[("a_act", a_m2)] if g == 0 else (),
                                   inc=("v_dve", 1) if g == 1 else None)
                p.emit("scalar", lambda e: e.activation(
                    ROWB[:], ROWC[:], AF.Ln, bias=EPS_LN),
                    wait=[("v_dve", v_var)])
                v_rs = p.emit("scalar", lambda e: e.activation(
                    ROWD[:], ROWB[:], AF.Exp, scale=-0.5), inc=("a_act", 1))
                t_bc = None
                for g in range(2):
                    gl = slice(g * 512, g * 512 + 512)
                    p.emit("tensor", lambda e, g=g, gl=gl: e.matmul(
                        mbp[g][:], ONES1[:], ROWA[0:1, gl], start=True, stop=True),
                        wait=[("a_act", v_rs)] if g == 0 else ())
                    t_bc = p.emit("tensor", lambda e, g=g, gl=gl: e.matmul(
                        rbp[g][:], ONES1[:], ROWD[0:1, gl], start=True, stop=True),
                        inc=("t_pe", 1) if g == 1 else None)
                v_done = [0] * 16
                for i in range(16):
                    db, g = i // 2, i % 2
                    sl = slice(db * 1024 + g * 512, db * 1024 + g * 512 + 512)
                    gl = slice(g * 512, g * 512 + 512)
                    def fnu(eng, sl=sl, g=g):
                        return eng.tensor_sub(TSC[:, g * 512:(g + 1) * 512],
                                              src[:, sl], mbp[g][:])
                    v_u = p.emit("vector", fnu,
                                 wait=[("t_pe", t_bc)] if i == 0 else (),
                                 inc=("v_dve", 1))
                    def fnx(eng, sl=sl, g=g):
                        return eng.tensor_mul(dst[:, sl],
                                              TSC[:, g * 512:(g + 1) * 512],
                                              rbp[g][:])
                    v_done[i] = p.emit("vector", fnx, wait=[("v_dve", v_u)],
                                       inc=("v_dve", 1))
                return v_done

            prev_vres1_15 = v_res[15]
            v_x2 = layer_norm(
                C, XBF, v_res,
                (P[0][0:1, :], P[1][0:1, :]), (P[0][32:33, :], P[1][32:33, :]),
                (P[0], P[1]), (P[2], P[3]),
                pe_war_wait=[("v_dve", v_res[15])])

            if dbg == "x2" and q == 0:
                p.emit("sync", lambda e: e.dma_start(out=DBG, in_=A[:]),
                    wait=[("v_dve", v_x2[15])], inc=("s_o", 16))
                out_dma_count += 1
                raise _Stop

            for tt in range(8):
                g = tt // 4
                w = [("v_dve", v_x2[14 + g])] if tt % 4 == 0 else []
                for k in range(8):
                    def fnr(eng, tt=tt, k=k):
                        return eng.matmul(
                            P[0][:, tt * 8:(tt + 1) * 8],
                            XBF[:, k * 1024 + tt * 128:k * 1024 + tt * 128 + 128],
                            RWS[:, k * 8:(k + 1) * 8],
                            start=(k == 0), stop=(k == 7))
                    t_rt = p.emit("tensor", fnr, wait=w,
                                  inc=("t_pe", 1) if (tt == 7 and k == 7) else None)
            a_log = p.emit("scalar", lambda e: e.copy(LOG[:], P[0][:, 0:64]),
                           wait=[("t_pe", t_rt)], inc=("a_act", 1))

            if dbg == "log" and q == 0:
                p.emit("sync", lambda e: e.dma_start(out=DBG[:, 0:64], in_=LOG[:]),
                    wait=[("a_act", a_log)], inc=("s_o", 16))
                out_dma_count += 1
                raise _Stop

            a_exp = [0] * 8
            v_ce = [0] * 8
            for tt in range(8):
                tl = slice(tt * 8, (tt + 1) * 8)
                sc = lambda j, tt=tt: ST[:, tt * 12 + j:tt * 12 + j + 1]
                v1 = p.emit("vector", lambda e, tl=tl, sc=sc: e.reduce_max(
                    sc(0), LOG[:, tl], axis=AX, negate=True),
                    wait=[("a_act", a_log)] if tt == 0 else (), inc=("v_dve", 1))
                a_exp[tt] = p.emit("scalar", lambda e, tl=tl, sc=sc: e.activation(
                    PRB[:, tl], LOG[:, tl], AF.Exp, bias=sc(0), scale=1.0,
                    accum_out=sc(1)),
                    wait=[("v_dve", v1)], inc=("a_act", 1))
                v2 = p.emit("vector", lambda e, sc=sc: e.reciprocal(sc(2), sc(1)),
                            wait=[("a_act", a_exp[tt])], inc=("v_dve", 1))
                v3 = p.emit("vector", lambda e, tl=tl, sc=sc: e.tensor_scalar(
                    PRB[:, tl], PRB[:, tl], sc(2), None, ALU.mult),
                    wait=[("v_dve", v2)], inc=("v_dve", 1))
                v4 = p.emit("vector", lambda e, tl=tl, sc=sc: e.reduce_max(
                    sc(3), PRB[:, tl], axis=AX),
                    wait=[("v_dve", v3)], inc=("v_dve", 1))
                v5 = p.emit("vector", lambda e, tl=tl, sc=sc: e.tensor_scalar(
                    MSK1[:, tl], PRB[:, tl], sc(3), None, ALU.is_ge),
                    wait=[("v_dve", v4)], inc=("v_dve", 1))
                v5b = p.emit("vector", lambda e, tl=tl: e.tensor_scalar(
                    NMS[:, tl], MSK1[:, tl], -1.0, 1.0, ALU.mult, ALU.add),
                    wait=[("v_dve", v5)], inc=("v_dve", 1))
                v6 = p.emit("vector", lambda e, tl=tl: e.tensor_mul(
                    PM[:, tl], PRB[:, tl], NMS[:, tl]),
                    wait=[("v_dve", v5b)], inc=("v_dve", 1))
                v7 = p.emit("vector", lambda e, tl=tl, sc=sc: e.reduce_max(
                    sc(4), PM[:, tl], axis=AX),
                    wait=[("v_dve", v6)], inc=("v_dve", 1))
                v8 = p.emit("vector", lambda e, tl=tl, sc=sc: e.tensor_scalar(
                    MSK2[:, tl], PM[:, tl], sc(4), None, ALU.is_ge),
                    wait=[("v_dve", v7)], inc=("v_dve", 1))
                v9 = p.emit("vector", lambda e, sc=sc: e.scalar_tensor_tensor(
                    sc(5), sc(3), 1.0, sc(4), ALU.mult, ALU.add),
                    wait=[("v_dve", v8)], inc=("v_dve", 1))
                a_d2 = p.emit("scalar", lambda e, sc=sc: e.activation(
                    sc(6), sc(5), AF.Copy, bias=EPS_RT),
                    wait=[("v_dve", v9)], inc=("a_act", 1))
                v10 = p.emit("vector", lambda e, sc=sc: e.reciprocal(sc(7), sc(6)),
                             wait=[("a_act", a_d2)], inc=("v_dve", 1))
                v10b = p.emit("vector", lambda e, sc=sc: e.tensor_mul(
                    sc(8), sc(3), sc(7)),
                    wait=[("v_dve", v10)], inc=("v_dve", 1))
                v11 = p.emit("vector", lambda e, sc=sc: e.tensor_mul(
                    sc(9), sc(4), sc(7)),
                    wait=[("v_dve", v10b)], inc=("v_dve", 1))
                v11b = p.emit("vector", lambda e, tl=tl, sc=sc: e.tensor_scalar(
                    CES[:, tl], MSK1[:, tl], sc(8), None, ALU.mult),
                    wait=[("v_dve", v11)], inc=("v_dve", 1))
                v11c = p.emit("vector", lambda e, tl=tl, sc=sc: e.tensor_scalar(
                    PM[:, tl], MSK2[:, tl], sc(9), None, ALU.mult),
                    wait=[("v_dve", v11b)], inc=("v_dve", 1))
                v_ce[tt] = p.emit("vector", lambda e, tl=tl: e.tensor_add(
                    CES[:, tl], CES[:, tl], PM[:, tl]),
                    wait=[("v_dve", v11c)], inc=("v_dve", 1))

            if dbg == "ce" and q == 0:
                p.emit("sync", lambda e: e.dma_start(out=DBG[:, 0:64], in_=CES[:]),
                    wait=[("v_dve", v_ce[7])], inc=("s_o", 16))
                out_dma_count += 1
                raise _Stop

            if dbg == "prb" and q == 0:
                p.emit("sync", lambda e: e.dma_start(out=DBG[:, 0:64], in_=PRB[:]),
                    wait=[("v_dve", v_ce[7])], inc=("s_o", 16))
                p.emit("sync", lambda e: e.dma_start(out=DBG[:, 64:128], in_=MSK1[:]),
                    inc=("s_o", 16))
                p.emit("sync", lambda e: e.dma_start(out=DBG[:, 128:224], in_=ST[:]),
                    inc=("s_o", 16))
                p.emit("sync", lambda e: e.dma_start(out=DBG[:, 224:288], in_=PM[:]),
                    inc=("s_o", 16))
                p.emit("sync", lambda e: e.dma_start(out=DBG[:, 288:352], in_=MSK2[:]),
                    inc=("s_o", 16))
                out_dma_count += 5
                raise _Stop

            v_acc = [0] * 8
            v_res2 = [0] * 16
            v_h = [0] * 16
            a_silu = [0] * 16
            prev_e_v_acc = None
            for e_ in range(ne):
                w13_g = [issue_w13(q, e_, c) for c in range(2)]
                w2_g = [issue_w2(q, e_, c) for c in range(2)]

                for pp in range(16):
                    gchunk = w13_g[pp]
                    slot = gchunk % 2
                    base = (P[0], P[1], P[2], P[3]) if pp % 2 == 0 else \
                           (P[4], P[5], P[6], P[7])
                    pg0, pg1, pu0, pu1 = base
                    w = [w13_loaded[gchunk]]
                    if pp >= 2:
                        w.append(("v_dve", v_h[pp - 2]))
                    elif pp == 1:
                        if e_ > 0:
                            w.append(("v_dve", v_h[15]))
                    else:
                        if e_ == 0:
                            w.append(("v_dve", v_x2[15]))
                            w.append(("a_act", a_log))
                        else:
                            w.append(("v_dve", prev_e_v_acc))
                    for k in range(8):
                        tb = slot * 2048 + k * 256
                        def fg(eng, tb=tb, pg0=pg0, pg1=pg1, k=k):
                            lh = WR13[:, tb:tb + 128]
                            eng.matmul(pg0[:], lh,
                                       XBF[:, k * 1024:k * 1024 + 512],
                                       start=(k == 0), stop=(k == 7))
                            return eng.matmul(pg1[:], lh,
                                              XBF[:, k * 1024 + 512:k * 1024 + 1024],
                                              start=(k == 0), stop=(k == 7))
                        p.emit("tensor", fg, wait=w if k == 0 else ())
                        def fu(eng, tb=tb, pu0=pu0, pu1=pu1, k=k):
                            lh = WR13[:, tb + 128:tb + 256]
                            eng.matmul(pu0[:], lh,
                                       XBF[:, k * 1024:k * 1024 + 512],
                                       start=(k == 0), stop=(k == 7))
                            return eng.matmul(pu1[:], lh,
                                              XBF[:, k * 1024 + 512:k * 1024 + 1024],
                                              start=(k == 0), stop=(k == 7))
                        t_pair = p.emit("tensor", fu,
                                        inc=("t_pe", 1) if k == 7 else None)
                    w13_consumed[gchunk] = t_pair
                    if pp + 2 < 16:
                        w13_g.append(issue_w13(q, e_, pp + 2))

                    wsl = [("t_pe", t_pair)]
                    if pp >= 1:
                        wsl.append(("v_dve", v_h[pp - 1]))
                    a_silu[pp] = None
                    for g in range(2):
                        pgx = pg0 if g == 0 else pg1
                        a_silu[pp] = p.emit("scalar", lambda e, g=g, pgx=pgx: e.activation(
                            SILU[:, g * 512:(g + 1) * 512], pgx[:], AF.Silu),
                            wait=wsl if g == 0 else (),
                            inc=("a_act", 1) if g == 1 else None)
                    for g in range(2):
                        pux = pu0 if g == 0 else pu1
                        def fh(eng, pp=pp, g=g, pux=pux):
                            return eng.tensor_mul(
                                H[:, pp * 1024 + g * 512:pp * 1024 + g * 512 + 512],
                                SILU[:, g * 512:(g + 1) * 512], pux[:])
                        v_h[pp] = p.emit("vector", fh,
                                         wait=[("a_act", a_silu[pp])] if g == 0 else (),
                                         inc=("v_dve", 1) if g == 1 else None)

                last_mm1_pe = p.cur("t_pe")

                if e_ == 0:
                    for tt in range(8):
                        dstp = P[5] if tt < 4 else P[6]
                        co = (tt % 4) * 128
                        wct = [("v_dve", v_ce[tt])]
                        if tt == 0:
                            wct += [("a_act", a_silu[15]), ("v_dve", v_h[15])]
                        def fnt(eng, tt=tt, dstp=dstp, co=co):
                            return eng.transpose(dstp[0:8, co:co + 128],
                                                 CES[:, tt * 8:(tt + 1) * 8],
                                                 IDS[:])
                        t_cet = p.emit("tensor", fnt, wait=wct,
                                       inc=("t_pe", 1) if tt == 7 else None)
                    p.emit("scalar", lambda e: e.copy(
                        CEF[0:8, 0:512], P[5][0:8, :]), wait=[("t_pe", t_cet)])
                    a_cef = p.emit("scalar", lambda e: e.copy(
                        CEF[0:8, 512:1024], P[6][0:8, :]), inc=("a_act", 1))

                wce = [("v_dve", v_h[15])]
                if e_ == 0:
                    wce.append(("a_act", a_cef))
                for g in range(2):
                    def fcb(eng, e_=e_, g=g):
                        return eng.matmul(
                            P[6 + g][:], OHES[:, e_ * 128:(e_ + 1) * 128],
                            CEF[0:8, g * 512:(g + 1) * 512],
                            start=True, stop=True)
                    t_ceb = p.emit("tensor", fcb, wait=wce if g == 0 else (),
                                   inc=("t_pe", 1) if g == 1 else None)
                a_ceb = None
                for g in range(2):
                    wcb = [("t_pe", t_ceb)]
                    if g == 0 and prev_e_v_acc is not None:
                        wcb.append(("v_dve", prev_e_v_acc))
                    a_ceb = p.emit("scalar", lambda e, g=g: e.copy(
                        CEB[:, g * 512:(g + 1) * 512], P[6 + g][:]),
                        wait=wcb if g == 0 else (),
                        inc=("a_act", 1) if g == 1 else None)

                for db in range(8):
                    gchunk = w2_g[db]
                    slot = gchunk % 2
                    py0 = P[0] if db % 2 == 0 else P[2]
                    py1 = P[1] if db % 2 == 0 else P[3]
                    w = [w2_loaded[gchunk], ("v_dve", v_h[15])]
                    if db >= 2:
                        w.append(("v_dve", v_acc[db - 2]))
                    for pp in range(16):
                        tb = slot * 2048 + pp * 128
                        def f2(eng, tb=tb, pp=pp, py0=py0, py1=py1):
                            lh = WR2[:, tb:tb + 128]
                            eng.matmul(py0[:], lh,
                                       H[:, pp * 1024:pp * 1024 + 512],
                                       start=(pp == 0), stop=(pp == 15))
                            return eng.matmul(py1[:], lh,
                                              H[:, pp * 1024 + 512:pp * 1024 + 1024],
                                              start=(pp == 0), stop=(pp == 15))
                        t_mm2 = p.emit("tensor", f2, wait=w if pp == 0 else (),
                                       inc=("t_pe", 1) if pp == 15 else None)
                    w2_consumed[gchunk] = t_mm2
                    if db + 2 < 8:
                        w2_g.append(issue_w2(q, e_, db + 2))

                    for g in range(2):
                        pyx = py0 if g == 0 else py1
                        sl = slice(db * 1024 + g * 512, db * 1024 + g * 512 + 512)
                        if e_ == 0:
                            def fa(eng, sl=sl, g=g, pyx=pyx):
                                return eng.tensor_mul(C[:, sl], pyx[:],
                                                      CEB[:, g * 512:(g + 1) * 512])
                            v_acc[db] = p.emit(
                                "vector", fa,
                                wait=[("t_pe", t_mm2), ("a_act", a_ceb)] if g == 0 else (),
                                inc=("v_dve", 1) if g == 1 else None)
                        else:
                            def fa1(eng, sl=sl, g=g, pyx=pyx):
                                return eng.tensor_mul(TSC[:, g * 512:(g + 1) * 512],
                                                      pyx[:],
                                                      CEB[:, g * 512:(g + 1) * 512])
                            v_a1 = p.emit("vector", fa1,
                                          wait=[("t_pe", t_mm2)] if g == 0 else (),
                                          inc=("v_dve", 1))
                            def fa2(eng, sl=sl, g=g):
                                return eng.tensor_add(C[:, sl], C[:, sl],
                                                      TSC[:, g * 512:(g + 1) * 512])
                            v_acc[db] = p.emit("vector", fa2, wait=[("v_dve", v_a1)],
                                               inc=("v_dve", 1) if g == 1 else None)
                    if e_ == ne - 1:
                        for g2 in range(2):
                            i2 = db * 2 + g2
                            sl2 = slice(db * 1024 + g2 * 512,
                                        db * 1024 + g2 * 512 + 512)
                            def fr(eng, sl2=sl2):
                                return eng.tensor_add(C[:, sl2], C[:, sl2],
                                                      XBF[:, sl2])
                            v_res2[i2] = p.emit("vector", fr, inc=("v_dve", 1))
                prev_e_v_acc = v_acc[7]
                if e_ == ne - 1:
                    last_ffn_pe = last_mm1_pe

            if dbg == "yf" and q == 0:
                p.emit("sync", lambda e: e.dma_start(out=DBG, in_=C[:]),
                    wait=[("v_dve", v_acc[7])], inc=("s_o", 16))
                out_dma_count += 1
                raise _Stop

            v_out = layer_norm(
                C, C, v_res2,
                (P[0][0:1, :], P[1][0:1, :]), (P[0][32:33, :], P[1][32:33, :]),
                (P[0], P[1]), (P[2], P[3]),
                pe_war_wait=[("v_dve", v_res2[15])])

            def fo(eng, b=b, s0=s0):
                return eng.dma_start(
                    out=OUT[b, :, s0:s0 + TOKQ].rearrange("(db p) s -> p db s", p=128),
                    in_=C[:].rearrange("p (db s) -> p db s", db=8))
            s_out_v = p.emit("sync", fo, wait=[("v_dve", v_out[15])],
                             inc=("s_o", 16))
            out_dma_count += 1

            if dbg is not None and q == dbg:
                def fd(eng):
                    return eng.dma_start(out=DBG, in_=C[:])
                p.emit("sync", fd, inc=("s_o", 16))
                out_dma_count += 1

            prev_ceb_a = a_ceb
            prev_vh15 = v_h[15]
            prev_vout15 = v_out[15]
            prev_vres2_15 = v_res2[15]
        except _Stop:
            pass

        p.emit("sync", lambda e: e.nop(), wait=[("s_o", 16 * out_dma_count)])

        with nc.Block() as block:
            @block.sync
            def _(eng):
                p.replay("sync", eng, sems)

            @block.gpsimd
            def _(eng):
                p.replay("gpsimd", eng, sems)

            @block.tensor
            def _(eng):
                p.replay("tensor", eng, sems)

            @block.scalar
            def _(eng):
                p.replay("scalar", eng, sems)

            @block.vector
            def _(eng):
                p.replay("vector", eng, sems)

    return nc


def _prep_inputs(L0_attn_x, L0_attn_W_lambda, L0_attn_b_lambda, L0_attn_W_B,
                 L0_attn_W_C, L0_attn_W_gate, L0_attn_rms_weight,
                 L0_ffn_router_w, L0_ffn_w13, L0_ffn_w2):
    x = np.asarray(L0_attn_x, dtype=np.float32)
    x_fm = np.ascontiguousarray(x.transpose(0, 2, 1))
    wl = np.asarray(L0_attn_W_lambda, np.float32)
    wb = np.asarray(L0_attn_W_B, np.float32)
    wlamb = np.ascontiguousarray(
        np.concatenate([wl.T, wb.T], axis=1))
    blam = np.ascontiguousarray(
        np.asarray(L0_attn_b_lambda, np.float32).reshape(NST, 1))
    wct = np.ascontiguousarray(np.asarray(L0_attn_W_C, np.float32).T)
    wgt = np.ascontiguousarray(np.asarray(L0_attn_W_gate, np.float32).T)
    rmsw = np.ascontiguousarray(
        np.asarray(L0_attn_rms_weight, np.float32).reshape(8, 128).T)
    rwt = np.ascontiguousarray(np.asarray(L0_ffn_router_w, np.float32).T)
    w13 = np.asarray(L0_ffn_w13, np.float32)
    arr = w13.reshape(NE, 8, 128, 32, 128).transpose(0, 3, 1, 2, 4)
    gate, up = arr[:, :16], arr[:, 16:]
    w13r = np.ascontiguousarray(
        np.stack([gate, up], axis=3)
        .reshape(NE, 256, 128, 128))
    w2 = np.asarray(L0_ffn_w2, np.float32)
    w2r = np.ascontiguousarray(
        w2.reshape(NE, 16, 128, 8, 128).transpose(0, 3, 1, 2, 4)
        .reshape(NE, 128, 128, 128))
    ohe = np.zeros((NE, 128 * NE), np.float32)
    for e in range(NE):
        ohe[e, e * 128:(e + 1) * 128] = 1.0
    ident = np.eye(128, dtype=np.float32)
    return x_fm, dict(
        wlamb=wlamb, blam=blam, wct=wct, wgt=wgt, rmsw=rmsw, rwt=rwt,
        w13r=w13r, w2r=w2r, ohe=ohe, ident=ident)


_NC_CACHE = {}


def kernel(**inputs):
    x_fm, shared = _prep_inputs(**inputs)
    key = "full"
    if key not in _NC_CACHE:
        _NC_CACHE[key] = build_nc()
    nc = _NC_CACHE[key]
    in_maps = []
    for core in range(8):
        m = dict(shared)
        m["x_fm"] = np.ascontiguousarray(x_fm[2 * core:2 * core + 2])
        in_maps.append(m)
    res = run_bass_kernel_spmd(nc, in_maps, core_ids=list(range(8)))
    outs = []
    for core in range(8):
        o = res.results[core]["out_fm"]
        outs.append(np.ascontiguousarray(o.transpose(0, 2, 1)))
    return np.concatenate(outs, axis=0).astype(np.float32)


# version 13 (final)
# speedup vs baseline: 1.0286x; 1.0286x over previous
import numpy as np
from contextlib import ExitStack

import concourse.bass as bass
import concourse.mybir as mybir
from concourse.bass_utils import run_bass_kernel_spmd

F32 = mybir.dt.float32
BF16 = mybir.dt.bfloat16
AF = mybir.ActivationFunctionType
ALU = mybir.AluOpType
AX = mybir.AxisListType.X

D = 1024
S = 2048
NB = 2
NQ = 4
TOKQ = 1024
NE = 8
NI = 2048
NST = 16
EPS_RMS = 1e-6
EPS_LN = 1e-5
EPS_RT = 1e-6


class Planner:
    ENGINES = ("sync", "gpsimd", "tensor", "scalar", "vector")

    def __init__(self):
        self.cmds = {e: [] for e in self.ENGINES}
        self.val = {}

    def emit(self, eng, fn, wait=(), inc=None):
        newv = None
        if inc is not None:
            sname, by = inc
            newv = self.val.get(sname, 0) + by
            self.val[sname] = newv
        self.cmds[eng].append((tuple(wait), fn, inc, newv))
        return newv

    def cur(self, sname):
        return self.val.get(sname, 0)

    def replay(self, eng_name, eng, sems):
        for wait, fn, inc, newv in self.cmds[eng_name]:
            for sname, v in wait:
                eng.wait_ge(sems[sname], v)
            ins = fn(eng)
            if inc is not None:
                assert ins is not None, f"inc {inc} on None instruction"
                ins.then_inc(sems[inc[0]], inc[1])


class _Stop(Exception):
    pass


def build_nc(nq=NQ, ne=NE, dbg=None):
    nc = bass.Bass("TRN2", target_bir_lowering=False)

    X = nc.dram_tensor("x_fm", [NB, D, S], F32, kind="ExternalInput").ap()
    WLB = nc.dram_tensor("wlamb", [D, 2 * NST], F32, kind="ExternalInput").ap()
    BLAM = nc.dram_tensor("blam", [NST, 1], F32, kind="ExternalInput").ap()
    WCT = nc.dram_tensor("wct", [NST, D], F32, kind="ExternalInput").ap()
    WGT = nc.dram_tensor("wgt", [D, D], F32, kind="ExternalInput").ap()
    RMSW = nc.dram_tensor("rmsw", [128, 8], F32, kind="ExternalInput").ap()
    RWT = nc.dram_tensor("rwt", [D, NE], F32, kind="ExternalInput").ap()
    W13R = nc.dram_tensor("w13r", [NE, 256, 128, 128], F32, kind="ExternalInput").ap()
    W2R = nc.dram_tensor("w2r", [NE, 128, 128, 128], F32, kind="ExternalInput").ap()
    OHE = nc.dram_tensor("ohe", [NE, 128 * NE], F32, kind="ExternalInput").ap()
    IDENT = nc.dram_tensor("ident", [128, 128], F32, kind="ExternalInput").ap()
    OUT = nc.dram_tensor("out_fm", [NB, D, S], F32, kind="ExternalOutput").ap()
    W13B = nc.dram_tensor("w13b_scr", [NE, 16, 128, 2048], BF16, kind="Internal").ap()
    W2B = nc.dram_tensor("w2b_scr", [NE, 8, 128, 2048], BF16, kind="Internal").ap()
    DBG = None
    if dbg is not None:
        DBG = nc.dram_tensor("dbg", [128, 8192], F32, kind="ExternalOutput").ap()

    p = Planner()

    with ExitStack() as st:
        sb = lambda n, s, d: st.enter_context(nc.sbuf_tensor(n, s, d))
        psa = lambda n, s, d: st.enter_context(nc.psum_tensor(n, s, d))

        A = sb("A", [128, 8192], F32)
        C = sb("C", [128, 8192], F32)
        H = sb("H", [128, 16384], BF16)
        XBF = sb("XBF", [128, 8192], BF16)
        LAMS = sb("LAMS", [NST, TOKQ], F32)
        BXS = sb("BXS", [NST, TOKQ], F32)
        HS0 = sb("HS0", [NST, TOKQ], F32)
        HS1 = sb("HS1", [NST, TOKQ], F32)
        HSBF = sb("HSBF", [NST, TOKQ], BF16)
        WGTS = sb("WGTS", [128, 8192], BF16)
        WLBS = sb("WLBS", [128, 8 * 32], BF16)
        WCTS = sb("WCTS", [NST, D], BF16)
        RWS = sb("RWS", [128, 8 * NE], BF16)
        RMSWS = sb("RMSWS", [128, 8], F32)
        BLAMS = sb("BLAMS", [NST, 1], F32)
        OHES = sb("OHES", [NE, 128 * NE], BF16)
        IDS = sb("IDS", [128, 128], F32)
        ONESB = sb("ONESB", [128, 1], BF16)
        ONES1 = sb("ONES1", [1, 128], F32)
        WR13 = sb("WR13", [128, 2 * 2048], BF16)
        WR2 = sb("WR2", [128, 2 * 2048], BF16)
        GTS = sb("GTS", [128, 1024], BF16)
        SILU = sb("SILU", [128, 1024], BF16)
        SQB = sb("SQB", [128, 2048], BF16)
        RBF = sb("RBF", [128, 1024], BF16)
        TSC = sb("TSC", [128, 1024], F32)
        CEB = sb("CEB", [128, 1024], F32)
        ROWA = sb("ROWA", [1, 1024], F32)
        ROWB = sb("ROWB", [1, 1024], F32)
        ROWC = sb("ROWC", [1, 1024], F32)
        ROWD = sb("ROWD", [1, 1024], F32)
        LOG = sb("LOG", [128, 64], F32)
        PRB = sb("PRB", [128, 64], F32)
        MSK1 = sb("MSK1", [128, 64], F32)
        MSK2 = sb("MSK2", [128, 64], F32)
        NMS = sb("NMS", [128, 64], F32)
        PM = sb("PM", [128, 64], F32)
        CES = sb("CES", [128, 64], F32)
        CEF = sb("CEF", [NE, TOKQ], BF16)
        ST = sb("ST", [128, 96], F32)

        P = [psa(f"P{i}", [128, 512], F32) for i in range(8)]

        sems = {}
        for sname in ("s_xp", "s_xq", "s_xb", "s_wp", "w13s0", "w13s1",
                      "w2s0", "w2s1", "w13w0", "w13w1", "w2w0", "w2w1",
                      "s_o", "t_pe", "a_act", "v_dve", "g_pl"):
            sems[sname] = st.enter_context(nc.semaphore(sname))

        p.emit("sync", lambda e: e.dma_start(out=RMSWS[:], in_=RMSW), inc=("s_xp", 16))
        p.emit("sync", lambda e: e.dma_start(out=BLAMS[:], in_=BLAM), inc=("s_xp", 16))
        p.emit("sync", lambda e: e.dma_start(out=IDS[:], in_=IDENT), inc=("s_xp", 16))
        p.emit("gpsimd", lambda e: e.dma_start(
            out=WLBS[:].rearrange("p (k n) -> p k n", k=8),
            in_=WLB.rearrange("(k p) n -> p k n", p=128)), inc=("s_wp", 16))
        p.emit("gpsimd", lambda e: e.dma_start(out=WCTS[:], in_=WCT), inc=("s_wp", 16))
        p.emit("gpsimd", lambda e: e.dma_start(
            out=WGTS[:].rearrange("p (k n) -> p k n", k=8),
            in_=WGT.rearrange("(k p) n -> p k n", p=128)), inc=("s_wp", 16))
        p.emit("gpsimd", lambda e: e.dma_start(
            out=RWS[:].rearrange("p (k n) -> p k n", k=8),
            in_=RWT.rearrange("(k p) n -> p k n", p=128)), inc=("s_wp", 16))
        p.emit("gpsimd", lambda e: e.dma_start(out=OHES[:], in_=OHE), inc=("s_wp", 16))
        prologue_w = p.cur("s_wp")
        prologue_x = p.cur("s_xp")
        p.emit("vector", lambda e: e.memset(ONESB[:], 1.0))
        p.emit("vector", lambda e: e.memset(ONES1[:], 1.0))
        CST = sb("CST", [128, 4], F32)
        p.emit("vector", lambda e: e.memset(CST[:, 0:1], EPS_RMS))
        p.emit("vector", lambda e: e.memset(CST[:, 1:2], EPS_LN))
        p.emit("vector", lambda e: e.memset(CST[:, 2:3], 1.0 / D))
        v_consts = p.emit("vector", lambda e: e.memset(CST[:, 3:4], -0.5),
                          inc=("v_dve", 1))
        nc.const_aps.register(F32, EPS_RMS, CST[:, 0:1])
        nc.const_aps.register(F32, EPS_LN, CST[:, 1:2])
        nc.const_aps.register(F32, 1.0 / D, CST[:, 2:3])
        nc.const_aps.register(F32, -0.5, CST[:, 3:4])

        w13_consumed = {}
        w2_consumed = {}
        w13_loaded = {}
        w2_loaded = {}
        w13_issued = 0
        w2_issued = 0

        wb_count = {"w13w0": 0, "w13w1": 0, "w2w0": 0, "w2w1": 0}
        w13_wbq = lambda e_: 0 if e_ < 5 else 1
        w13_done = {}
        w2_done = {}

        def issue_w13(q, e_, chunk):
            nonlocal w13_issued
            g = w13_issued
            slot = g % 2
            ssem = "w13s%d" % slot
            wsem = "w13w%d" % slot
            cached = w13_done.get(e_, False)
            do_wb = (not cached) and q == w13_wbq(e_)
            wait = []
            if g >= 2:
                wait.append(("t_pe", w13_consumed[g - 2]))
                if wb_count[wsem]:
                    wait.append((wsem, 16 * wb_count[wsem]))
            if not cached and not do_wb and q > w13_wbq(e_):
                cached = True
            if cached:
                def fn(eng, e_=e_, chunk=chunk, slot=slot):
                    return eng.dma_start(
                        out=WR13[:, slot * 2048:(slot + 1) * 2048],
                        in_=W13B[e_, chunk])
                fill = p.emit("gpsimd", fn, wait=wait, inc=(ssem, 16))
            else:
                def fn(eng, e_=e_, chunk=chunk, slot=slot):
                    return eng.dma_start(
                        out=WR13[:, slot * 2048:(slot + 1) * 2048].rearrange(
                            "p (t f) -> p t f", t=16),
                        in_=W13R[e_, chunk * 16:(chunk + 1) * 16].rearrange(
                            "t p f -> p t f"))
                fill = p.emit("gpsimd", fn, wait=wait, inc=(ssem, 16))
            w13_loaded[g] = (ssem, fill)
            if do_wb:
                def fw(eng, e_=e_, chunk=chunk, slot=slot):
                    return eng.dma_start(
                        out=W13B[e_, chunk],
                        in_=WR13[:, slot * 2048:(slot + 1) * 2048])
                p.emit("gpsimd", fw, wait=[(ssem, fill)], inc=(wsem, 16))
                wb_count[wsem] += 1
                if chunk == 15:
                    w13_done[e_] = True
            w13_issued += 1
            return g

        def issue_w2(q, e_, chunk):
            nonlocal w2_issued
            g = w2_issued
            slot = g % 2
            ssem = "w2s%d" % slot
            wsem = "w2w%d" % slot
            cached = w2_done.get(e_, False)
            do_wb = (not cached) and q == 1
            wait = []
            if g >= 2:
                wait.append(("t_pe", w2_consumed[g - 2]))
                if wb_count[wsem]:
                    wait.append((wsem, 16 * wb_count[wsem]))
            if cached:
                def fn(eng, e_=e_, chunk=chunk, slot=slot):
                    return eng.dma_start(
                        out=WR2[:, slot * 2048:(slot + 1) * 2048],
                        in_=W2B[e_, chunk])
                fill = p.emit("gpsimd", fn, wait=wait, inc=(ssem, 16))
            else:
                def fn(eng, e_=e_, chunk=chunk, slot=slot):
                    return eng.dma_start(
                        out=WR2[:, slot * 2048:(slot + 1) * 2048].rearrange(
                            "p (t f) -> p t f", t=16),
                        in_=W2R[e_, chunk * 16:(chunk + 1) * 16].rearrange(
                            "t p f -> p t f"))
                fill = p.emit("gpsimd", fn, wait=wait, inc=(ssem, 16))
            w2_loaded[g] = (ssem, fill)
            if do_wb:
                def fw(eng, e_=e_, chunk=chunk, slot=slot):
                    return eng.dma_start(
                        out=W2B[e_, chunk],
                        in_=WR2[:, slot * 2048:(slot + 1) * 2048])
                p.emit("gpsimd", fw, wait=[(ssem, fill)], inc=(wsem, 16))
                wb_count[wsem] += 1
                if chunk == 7:
                    w2_done[e_] = True
            w2_issued += 1
            return g

        last_ffn_pe = 0
        out_dma_count = 0
        prev_ceb_a = None
        prev_vh15 = None
        prev_vout15 = None
        prev_vres2_15 = None
        prev_vres1_15 = None

        try:
          for q in range(nq):
            b = q // 2
            s0 = (q % 2) * TOKQ
            HSq = HS0 if q % 2 == 0 else HS1
            HSprev = HS1 if q % 2 == 0 else HS0

            xw = [("s_xq", 16 * q)]
            if prev_vres1_15 is not None:
                xw.append(("v_dve", prev_vres1_15))
            xq_val = p.emit("scalar", lambda e, b=b, s0=s0: e.dma_start(
                out=A[:].rearrange("p (db s) -> p db s", db=8),
                in_=X[b, :, s0:s0 + TOKQ].rearrange("(db p) s -> p db s", p=128)),
                wait=xw, inc=("s_xq", 16))

            w = [("s_xb", 16 * q)]
            if last_ffn_pe:
                w.append(("t_pe", last_ffn_pe))
            if prev_vres2_15 is not None:
                w.append(("v_dve", prev_vres2_15))
            xbf_val = p.emit("gpsimd", lambda e, b=b, s0=s0: e.dma_start(
                out=XBF[:].rearrange("p (db s) -> p db s", db=8),
                in_=X[b, :, s0:s0 + TOKQ].rearrange("(db p) s -> p db s", p=128)),
                wait=w, inc=("s_xb", 16))

            wlam = [("s_xb", xbf_val), ("s_wp", prologue_w)]
            if prev_ceb_a is not None:
                wlam += [("a_act", prev_ceb_a), ("v_dve", prev_vh15)]
            for g in range(2):
                for k in range(8):
                    def fn(eng, g=g, k=k):
                        return eng.matmul(
                            P[4 + g][0:NST, :],
                            WLBS[:, k * 32:k * 32 + NST],
                            XBF[:, k * 1024 + g * 512:k * 1024 + g * 512 + 512],
                            start=(k == 0), stop=(k == 7))
                    p.emit("tensor", fn, wait=wlam if (g == 0 and k == 0) else ())
            for g in range(2):
                for k in range(8):
                    def fn(eng, g=g, k=k):
                        return eng.matmul(
                            P[6 + g][0:NST, :],
                            WLBS[:, k * 32 + NST:k * 32 + 32],
                            XBF[:, k * 1024 + g * 512:k * 1024 + g * 512 + 512],
                            start=(k == 0), stop=(k == 7))
                    is_last = (g == 1 and k == 7)
                    t_lam = p.emit("tensor", fn,
                                   inc=("t_pe", 1) if is_last else None)

            wv = [("t_pe", t_lam), ("s_xp", prologue_x), ("v_dve", v_consts)]
            for g in range(2):
                p.emit("scalar", lambda e, g=g: e.activation(
                    LAMS[:, g * 512:(g + 1) * 512], P[4 + g][0:NST, :],
                    AF.Sigmoid, bias=BLAMS[:], scale=1.0),
                    wait=wv if g == 0 else ())
            for g in range(2):
                a_lam = p.emit("scalar", lambda e, g=g: e.copy(
                    BXS[:, g * 512:(g + 1) * 512], P[6 + g][0:NST, :]),
                    inc=("a_act", 1) if g == 1 else None)

            init = 0.0 if q % 2 == 0 else HSprev[:, TOKQ - 1:TOKQ]
            v_scan = p.emit("vector", lambda e, init=init, HSq=HSq: e.tensor_tensor_scan(
                HSq[:], LAMS[:], BXS[:], init, ALU.mult, ALU.add),
                wait=[("a_act", a_lam)], inc=("v_dve", 1))
            a_hsbf = p.emit("scalar", lambda e, HSq=HSq: e.copy(HSBF[:], HSq[:]),
                            wait=[("v_dve", v_scan)], inc=("a_act", 1))
            if dbg == "hs" and q == 0:
                p.emit("sync", lambda e, HSq=HSq: e.dma_start(
                    out=DBG[0:NST, 0:TOKQ], in_=HSq[:]),
                    wait=[("v_dve", v_scan)], inc=("s_o", 16))
                out_dma_count += 1
                raise _Stop

            t_gy = [0] * 16
            a_g = [0] * 16
            v_yg = [0] * 16
            a_sq = [0] * 16
            t_rs_i = [0] * 16

            def emit_rsum(j):
                db, g = j // 2, j % 2
                wr = [("a_act", a_sq[j])]
                if j == 0 and prev_vout15 is not None:
                    wr.append(("v_dve", prev_vout15))
                def fnr(eng, db=db, g=g, j=j):
                    return eng.matmul(
                        P[g][0:1, :], ONESB[:],
                        SQB[:, (j % 4) * 512:(j % 4) * 512 + 512],
                        start=(db == 0), stop=(db == 7))
                t_rs_i[j] = p.emit("tensor", fnr, wait=wr,
                                   inc=("t_pe", 1))

            for i in range(16):
                db, g = i // 2, i % 2
                pg = P[4 + (i % 2)]
                py = P[6 + (i % 2)]
                w = []
                if i == 0:
                    w.append(("a_act", a_hsbf))
                if i >= 2:
                    w.append(("a_act", a_g[i - 2]))
                    w.append(("v_dve", v_yg[i - 2]))
                for k in range(8):
                    def fn(eng, db=db, g=g, k=k, pg=pg):
                        return eng.matmul(
                            pg[:],
                            WGTS[:, k * 1024 + db * 128:k * 1024 + db * 128 + 128],
                            XBF[:, k * 1024 + g * 512:k * 1024 + g * 512 + 512],
                            start=(k == 0), stop=(k == 7))
                    p.emit("tensor", fn, wait=w if k == 0 else ())
                def fny(eng, db=db, g=g, py=py):
                    return eng.matmul(
                        py[:],
                        WCTS[0:NST, db * 128:(db + 1) * 128],
                        HSBF[0:NST, g * 512:(g + 1) * 512],
                        start=True, stop=True)
                t_gy[i] = p.emit("tensor", fny, inc=("t_pe", 1))
                if i >= 2:
                    emit_rsum(i - 2)
                w = [("t_pe", t_gy[i])]
                if i >= 2:
                    w.append(("v_dve", v_yg[i - 2]))
                a_g[i] = p.emit("scalar", lambda e, g=g, pg=pg: e.activation(
                    GTS[:, g * 512:(g + 1) * 512], pg[:], AF.Sigmoid),
                    wait=w, inc=("a_act", 1))
                wyg = [("a_act", a_g[i]), ("t_pe", t_gy[i])]
                if i == 0 and q > 0:
                    wyg.append(("s_o", p.cur("s_o")))
                v_yg[i] = p.emit("vector", lambda e, db=db, g=g, py=py: e.tensor_mul(
                    C[:, db * 1024 + g * 512:db * 1024 + g * 512 + 512],
                    GTS[:, g * 512:(g + 1) * 512], py[:]),
                    wait=wyg, inc=("v_dve", 1))
                ws = [("v_dve", v_yg[i])]
                if i >= 4:
                    ws.append(("t_pe", t_rs_i[i - 4]))
                a_sq[i] = p.emit("scalar", lambda e, db=db, g=g, i=i: e.activation(
                    SQB[:, (i % 4) * 512:(i % 4) * 512 + 512],
                    C[:, db * 1024 + g * 512:db * 1024 + g * 512 + 512],
                    AF.Square),
                    wait=ws, inc=("a_act", 1))
            emit_rsum(14)
            emit_rsum(15)
            t_rs = t_rs_i[15]

            if dbg == "yg" and q == 0:
                p.emit("sync", lambda e: e.dma_start(out=DBG, in_=C[:]),
                    wait=[("v_dve", v_yg[15])], inc=("s_o", 16))
                out_dma_count += 1
                raise _Stop

            a_rr = None
            for g in range(2):
                p.emit("scalar", lambda e, g=g: e.activation(
                    ROWC[0:1, g * 512:(g + 1) * 512], P[g][0:1, :],
                    AF.Ln, bias=EPS_RMS, scale=1.0 / D),
                    wait=[("t_pe", t_rs)] if g == 0 else ())
            a_rr = p.emit("scalar", lambda e: e.activation(
                ROWD[:], ROWC[:], AF.Exp, scale=-0.5), inc=("a_act", 1))
            for g in range(2):
                wrb = [("a_act", a_rr)]
                if g == 0 and prev_vout15 is not None:
                    wrb.append(("v_dve", prev_vout15))
                t_rb = p.emit("tensor", lambda e, g=g: e.matmul(
                    P[2 + g][:], ONES1[:], ROWD[0:1, g * 512:(g + 1) * 512],
                    start=True, stop=True),
                    wait=wrb if g == 0 else (),
                    inc=("t_pe", 1) if g == 1 else None)

            v_res = [0] * 16
            for i in range(16):
                db, g = i // 2, i % 2
                def fn1(eng, db=db, g=g):
                    return eng.scalar_tensor_tensor(
                        TSC[:, g * 512:(g + 1) * 512],
                        C[:, db * 1024 + g * 512:db * 1024 + g * 512 + 512],
                        RMSWS[:, db:db + 1], P[2 + g][:],
                        ALU.mult, ALU.mult)
                v_t1 = p.emit("vector", fn1,
                              wait=[("t_pe", t_rb)] if i == 0 else (),
                              inc=("v_dve", 1))
                def fn2(eng, db=db, g=g):
                    return eng.tensor_add(
                        C[:, db * 1024 + g * 512:db * 1024 + g * 512 + 512],
                        TSC[:, g * 512:(g + 1) * 512],
                        A[:, db * 1024 + g * 512:db * 1024 + g * 512 + 512])
                v_res[i] = p.emit("vector", fn2, wait=[("v_dve", v_t1)],
                                  inc=("v_dve", 1))

            if dbg == "r1" and q == 0:
                p.emit("sync", lambda e: e.dma_start(out=DBG, in_=C[:]),
                    wait=[("v_dve", v_res[15])], inc=("s_o", 16))
                out_dma_count += 1
                raise _Stop

            def layer_norm(src, dst, v_src, s1p, s2p, mbp, rbp, pe_war_wait):
                t_sum_i = [0] * 16
                t_sum = None
                for i in range(16):
                    db, g = i // 2, i % 2
                    sl = slice(db * 1024 + g * 512, db * 1024 + g * 512 + 512)
                    gl = slice(g * 512, g * 512 + 512)
                    w = [("v_dve", v_src[i])]
                    if i >= 2:
                        w.append(("t_pe", t_sum_i[i - 2]))
                    p.emit("scalar", lambda e, sl=sl, gl=gl: e.copy(
                        RBF[:, gl], src[:, sl]), wait=w)
                    a_cast = p.emit("scalar", lambda e, sl=sl, gl=gl: e.activation(
                        SQB[:, gl], src[:, sl], AF.Square), inc=("a_act", 1))
                    wp = [("a_act", a_cast)]
                    if i == 0:
                        wp += pe_war_wait
                    def fns1(eng, db=db, g=g, gl=gl):
                        return eng.matmul(s1p[g], ONESB[:], RBF[:, gl],
                                          start=(db == 0), stop=(db == 7))
                    p.emit("tensor", fns1, wait=wp)
                    def fns2(eng, db=db, g=g, gl=gl):
                        return eng.matmul(s2p[g], ONESB[:], SQB[:, gl],
                                          start=(db == 0), stop=(db == 7))
                    t_sum_i[i] = p.emit("tensor", fns2, inc=("t_pe", 1))
                t_sum = t_sum_i[15]
                a_m = p.emit("scalar", lambda e: e.activation(
                    ROWA[0:1, 0:512], s1p[0], AF.Copy, scale=1.0 / D),
                    wait=[("t_pe", t_sum)])
                a_m = p.emit("scalar", lambda e: e.activation(
                    ROWA[0:1, 512:1024], s1p[1], AF.Copy, scale=1.0 / D),
                    inc=("a_act", 1))
                a_m2 = p.emit("scalar", lambda e: e.activation(
                    ROWB[:], ROWA[:], AF.Square), wait=[("a_act", a_m)],
                    inc=("a_act", 1))
                v_var = None
                for g in range(2):
                    gl = slice(g * 512, g * 512 + 512)
                    def fnv(eng, g=g, gl=gl):
                        return eng.scalar_tensor_tensor(
                            ROWC[0:1, gl], s2p[g], 1.0 / D, ROWB[0:1, gl],
                            ALU.mult, ALU.subtract)
                    v_var = p.emit("vector", fnv,
                                   wait=[("a_act", a_m2)] if g == 0 else (),
                                   inc=("v_dve", 1) if g == 1 else None)
                p.emit("scalar", lambda e: e.activation(
                    ROWB[:], ROWC[:], AF.Ln, bias=EPS_LN),
                    wait=[("v_dve", v_var)])
                v_rs = p.emit("scalar", lambda e: e.activation(
                    ROWD[:], ROWB[:], AF.Exp, scale=-0.5), inc=("a_act", 1))
                t_bc = None
                for g in range(2):
                    gl = slice(g * 512, g * 512 + 512)
                    p.emit("tensor", lambda e, g=g, gl=gl: e.matmul(
                        mbp[g][:], ONES1[:], ROWA[0:1, gl], start=True, stop=True),
                        wait=[("a_act", v_rs)] if g == 0 else ())
                    t_bc = p.emit("tensor", lambda e, g=g, gl=gl: e.matmul(
                        rbp[g][:], ONES1[:], ROWD[0:1, gl], start=True, stop=True),
                        inc=("t_pe", 1) if g == 1 else None)
                v_done = [0] * 16
                for i in range(16):
                    db, g = i // 2, i % 2
                    sl = slice(db * 1024 + g * 512, db * 1024 + g * 512 + 512)
                    gl = slice(g * 512, g * 512 + 512)
                    def fnu(eng, sl=sl, g=g):
                        return eng.tensor_sub(TSC[:, g * 512:(g + 1) * 512],
                                              src[:, sl], mbp[g][:])
                    v_u = p.emit("vector", fnu,
                                 wait=[("t_pe", t_bc)] if i == 0 else (),
                                 inc=("v_dve", 1))
                    def fnx(eng, sl=sl, g=g):
                        return eng.tensor_mul(dst[:, sl],
                                              TSC[:, g * 512:(g + 1) * 512],
                                              rbp[g][:])
                    v_done[i] = p.emit("vector", fnx, wait=[("v_dve", v_u)],
                                       inc=("v_dve", 1))
                return v_done

            prev_vres1_15 = v_res[15]
            v_x2 = layer_norm(
                C, XBF, v_res,
                (P[0][0:1, :], P[1][0:1, :]), (P[0][32:33, :], P[1][32:33, :]),
                (P[0], P[1]), (P[2], P[3]),
                pe_war_wait=[("v_dve", v_res[15])])

            if dbg == "x2" and q == 0:
                p.emit("sync", lambda e: e.dma_start(out=DBG, in_=A[:]),
                    wait=[("v_dve", v_x2[15])], inc=("s_o", 16))
                out_dma_count += 1
                raise _Stop

            for tt in range(8):
                g = tt // 4
                w = [("v_dve", v_x2[14 + g])] if tt % 4 == 0 else []
                for k in range(8):
                    def fnr(eng, tt=tt, k=k):
                        return eng.matmul(
                            P[0][:, tt * 8:(tt + 1) * 8],
                            XBF[:, k * 1024 + tt * 128:k * 1024 + tt * 128 + 128],
                            RWS[:, k * 8:(k + 1) * 8],
                            start=(k == 0), stop=(k == 7))
                    t_rt = p.emit("tensor", fnr, wait=w,
                                  inc=("t_pe", 1) if (tt == 7 and k == 7) else None)
            a_log = p.emit("scalar", lambda e: e.copy(LOG[:], P[0][:, 0:64]),
                           wait=[("t_pe", t_rt)], inc=("a_act", 1))

            if dbg == "log" and q == 0:
                p.emit("sync", lambda e: e.dma_start(out=DBG[:, 0:64], in_=LOG[:]),
                    wait=[("a_act", a_log)], inc=("s_o", 16))
                out_dma_count += 1
                raise _Stop

            a_exp = [0] * 8
            v_ce = [0] * 8
            for tt in range(8):
                tl = slice(tt * 8, (tt + 1) * 8)
                sc = lambda j, tt=tt: ST[:, tt * 12 + j:tt * 12 + j + 1]
                v1 = p.emit("vector", lambda e, tl=tl, sc=sc: e.reduce_max(
                    sc(0), LOG[:, tl], axis=AX, negate=True),
                    wait=[("a_act", a_log)] if tt == 0 else (), inc=("v_dve", 1))
                a_exp[tt] = p.emit("scalar", lambda e, tl=tl, sc=sc: e.activation(
                    PRB[:, tl], LOG[:, tl], AF.Exp, bias=sc(0), scale=1.0,
                    accum_out=sc(1)),
                    wait=[("v_dve", v1)], inc=("a_act", 1))
                v2 = p.emit("vector", lambda e, sc=sc: e.reciprocal(sc(2), sc(1)),
                            wait=[("a_act", a_exp[tt])], inc=("v_dve", 1))
                v3 = p.emit("vector", lambda e, tl=tl, sc=sc: e.tensor_scalar(
                    PRB[:, tl], PRB[:, tl], sc(2), None, ALU.mult),
                    wait=[("v_dve", v2)], inc=("v_dve", 1))
                v4 = p.emit("vector", lambda e, tl=tl, sc=sc: e.reduce_max(
                    sc(3), PRB[:, tl], axis=AX),
                    wait=[("v_dve", v3)], inc=("v_dve", 1))
                v5 = p.emit("vector", lambda e, tl=tl, sc=sc: e.tensor_scalar(
                    MSK1[:, tl], PRB[:, tl], sc(3), None, ALU.is_ge),
                    wait=[("v_dve", v4)], inc=("v_dve", 1))
                v5b = p.emit("vector", lambda e, tl=tl: e.tensor_scalar(
                    NMS[:, tl], MSK1[:, tl], -1.0, 1.0, ALU.mult, ALU.add),
                    wait=[("v_dve", v5)], inc=("v_dve", 1))
                v6 = p.emit("vector", lambda e, tl=tl: e.tensor_mul(
                    PM[:, tl], PRB[:, tl], NMS[:, tl]),
                    wait=[("v_dve", v5b)], inc=("v_dve", 1))
                v7 = p.emit("vector", lambda e, tl=tl, sc=sc: e.reduce_max(
                    sc(4), PM[:, tl], axis=AX),
                    wait=[("v_dve", v6)], inc=("v_dve", 1))
                v8 = p.emit("vector", lambda e, tl=tl, sc=sc: e.tensor_scalar(
                    MSK2[:, tl], PM[:, tl], sc(4), None, ALU.is_ge),
                    wait=[("v_dve", v7)], inc=("v_dve", 1))
                v9 = p.emit("vector", lambda e, sc=sc: e.scalar_tensor_tensor(
                    sc(5), sc(3), 1.0, sc(4), ALU.mult, ALU.add),
                    wait=[("v_dve", v8)], inc=("v_dve", 1))
                a_d2 = p.emit("scalar", lambda e, sc=sc: e.activation(
                    sc(6), sc(5), AF.Copy, bias=EPS_RT),
                    wait=[("v_dve", v9)], inc=("a_act", 1))
                v10 = p.emit("vector", lambda e, sc=sc: e.reciprocal(sc(7), sc(6)),
                             wait=[("a_act", a_d2)], inc=("v_dve", 1))
                v10b = p.emit("vector", lambda e, sc=sc: e.tensor_mul(
                    sc(8), sc(3), sc(7)),
                    wait=[("v_dve", v10)], inc=("v_dve", 1))
                v11 = p.emit("vector", lambda e, sc=sc: e.tensor_mul(
                    sc(9), sc(4), sc(7)),
                    wait=[("v_dve", v10b)], inc=("v_dve", 1))
                v11b = p.emit("vector", lambda e, tl=tl, sc=sc: e.tensor_scalar(
                    CES[:, tl], MSK1[:, tl], sc(8), None, ALU.mult),
                    wait=[("v_dve", v11)], inc=("v_dve", 1))
                v11c = p.emit("vector", lambda e, tl=tl, sc=sc: e.tensor_scalar(
                    PM[:, tl], MSK2[:, tl], sc(9), None, ALU.mult),
                    wait=[("v_dve", v11b)], inc=("v_dve", 1))
                v_ce[tt] = p.emit("vector", lambda e, tl=tl: e.tensor_add(
                    CES[:, tl], CES[:, tl], PM[:, tl]),
                    wait=[("v_dve", v11c)], inc=("v_dve", 1))

            if dbg == "ce" and q == 0:
                p.emit("sync", lambda e: e.dma_start(out=DBG[:, 0:64], in_=CES[:]),
                    wait=[("v_dve", v_ce[7])], inc=("s_o", 16))
                out_dma_count += 1
                raise _Stop

            if dbg == "prb" and q == 0:
                p.emit("sync", lambda e: e.dma_start(out=DBG[:, 0:64], in_=PRB[:]),
                    wait=[("v_dve", v_ce[7])], inc=("s_o", 16))
                p.emit("sync", lambda e: e.dma_start(out=DBG[:, 64:128], in_=MSK1[:]),
                    inc=("s_o", 16))
                p.emit("sync", lambda e: e.dma_start(out=DBG[:, 128:224], in_=ST[:]),
                    inc=("s_o", 16))
                p.emit("sync", lambda e: e.dma_start(out=DBG[:, 224:288], in_=PM[:]),
                    inc=("s_o", 16))
                p.emit("sync", lambda e: e.dma_start(out=DBG[:, 288:352], in_=MSK2[:]),
                    inc=("s_o", 16))
                out_dma_count += 5
                raise _Stop

            v_acc = [0] * 8
            v_res2 = [0] * 16
            v_h = [0] * 16
            a_silu = [0] * 16
            prev_e_v_acc = None
            for e_ in range(ne):
                w13_g = [issue_w13(q, e_, c) for c in range(2)]
                w2_g = [issue_w2(q, e_, c) for c in range(2)]

                for pp in range(16):
                    gchunk = w13_g[pp]
                    slot = gchunk % 2
                    base = (P[0], P[1], P[2], P[3]) if pp % 2 == 0 else \
                           (P[4], P[5], P[6], P[7])
                    pg0, pg1, pu0, pu1 = base
                    w = [w13_loaded[gchunk]]
                    if pp >= 2:
                        w.append(("v_dve", v_h[pp - 2]))
                    elif pp == 1:
                        if e_ > 0:
                            w.append(("v_dve", v_h[15]))
                    else:
                        if e_ == 0:
                            w.append(("v_dve", v_x2[15]))
                            w.append(("a_act", a_log))
                        else:
                            w.append(("v_dve", prev_e_v_acc))
                    for k in range(8):
                        tb = slot * 2048 + k * 256
                        def fg(eng, tb=tb, pg0=pg0, pg1=pg1, k=k):
                            lh = WR13[:, tb:tb + 128]
                            eng.matmul(pg0[:], lh,
                                       XBF[:, k * 1024:k * 1024 + 512],
                                       start=(k == 0), stop=(k == 7))
                            return eng.matmul(pg1[:], lh,
                                              XBF[:, k * 1024 + 512:k * 1024 + 1024],
                                              start=(k == 0), stop=(k == 7))
                        p.emit("tensor", fg, wait=w if k == 0 else ())
                        def fu(eng, tb=tb, pu0=pu0, pu1=pu1, k=k):
                            lh = WR13[:, tb + 128:tb + 256]
                            eng.matmul(pu0[:], lh,
                                       XBF[:, k * 1024:k * 1024 + 512],
                                       start=(k == 0), stop=(k == 7))
                            return eng.matmul(pu1[:], lh,
                                              XBF[:, k * 1024 + 512:k * 1024 + 1024],
                                              start=(k == 0), stop=(k == 7))
                        t_pair = p.emit("tensor", fu,
                                        inc=("t_pe", 1) if k == 7 else None)
                    w13_consumed[gchunk] = t_pair
                    if pp + 2 < 16:
                        w13_g.append(issue_w13(q, e_, pp + 2))

                    wsl = [("t_pe", t_pair)]
                    if pp >= 1:
                        wsl.append(("v_dve", v_h[pp - 1]))
                    a_silu[pp] = None
                    for g in range(2):
                        pgx = pg0 if g == 0 else pg1
                        a_silu[pp] = p.emit("scalar", lambda e, g=g, pgx=pgx: e.activation(
                            SILU[:, g * 512:(g + 1) * 512], pgx[:], AF.Silu),
                            wait=wsl if g == 0 else (),
                            inc=("a_act", 1) if g == 1 else None)
                    for g in range(2):
                        pux = pu0 if g == 0 else pu1
                        def fh(eng, pp=pp, g=g, pux=pux):
                            return eng.tensor_mul(
                                H[:, pp * 1024 + g * 512:pp * 1024 + g * 512 + 512],
                                SILU[:, g * 512:(g + 1) * 512], pux[:])
                        v_h[pp] = p.emit("vector", fh,
                                         wait=[("a_act", a_silu[pp])] if g == 0 else (),
                                         inc=("v_dve", 1) if g == 1 else None)

                last_mm1_pe = p.cur("t_pe")

                if e_ == 0:
                    for tt in range(8):
                        dstp = P[5] if tt < 4 else P[6]
                        co = (tt % 4) * 128
                        wct = [("v_dve", v_ce[tt])]
                        if tt == 0:
                            wct += [("a_act", a_silu[15]), ("v_dve", v_h[15])]
                        def fnt(eng, tt=tt, dstp=dstp, co=co):
                            return eng.transpose(dstp[0:8, co:co + 128],
                                                 CES[:, tt * 8:(tt + 1) * 8],
                                                 IDS[:])
                        t_cet = p.emit("tensor", fnt, wait=wct,
                                       inc=("t_pe", 1) if tt == 7 else None)
                    p.emit("scalar", lambda e: e.copy(
                        CEF[0:8, 0:512], P[5][0:8, :]), wait=[("t_pe", t_cet)])
                    a_cef = p.emit("scalar", lambda e: e.copy(
                        CEF[0:8, 512:1024], P[6][0:8, :]), inc=("a_act", 1))

                wce = [("v_dve", v_h[15])]
                if e_ == 0:
                    wce.append(("a_act", a_cef))
                for g in range(2):
                    def fcb(eng, e_=e_, g=g):
                        return eng.matmul(
                            P[6 + g][:], OHES[:, e_ * 128:(e_ + 1) * 128],
                            CEF[0:8, g * 512:(g + 1) * 512],
                            start=True, stop=True)
                    t_ceb = p.emit("tensor", fcb, wait=wce if g == 0 else (),
                                   inc=("t_pe", 1) if g == 1 else None)
                a_ceb = None
                for g in range(2):
                    wcb = [("t_pe", t_ceb)]
                    if g == 0 and prev_e_v_acc is not None:
                        wcb.append(("v_dve", prev_e_v_acc))
                    a_ceb = p.emit("scalar", lambda e, g=g: e.copy(
                        CEB[:, g * 512:(g + 1) * 512], P[6 + g][:]),
                        wait=wcb if g == 0 else (),
                        inc=("a_act", 1) if g == 1 else None)

                for db in range(8):
                    gchunk = w2_g[db]
                    slot = gchunk % 2
                    py0 = P[0] if db % 2 == 0 else P[2]
                    py1 = P[1] if db % 2 == 0 else P[3]
                    w = [w2_loaded[gchunk], ("v_dve", v_h[15])]
                    if db >= 2:
                        w.append(("v_dve", v_acc[db - 2]))
                    for pp in range(16):
                        tb = slot * 2048 + pp * 128
                        def f2(eng, tb=tb, pp=pp, py0=py0, py1=py1):
                            lh = WR2[:, tb:tb + 128]
                            eng.matmul(py0[:], lh,
                                       H[:, pp * 1024:pp * 1024 + 512],
                                       start=(pp == 0), stop=(pp == 15))
                            return eng.matmul(py1[:], lh,
                                              H[:, pp * 1024 + 512:pp * 1024 + 1024],
                                              start=(pp == 0), stop=(pp == 15))
                        t_mm2 = p.emit("tensor", f2, wait=w if pp == 0 else (),
                                       inc=("t_pe", 1) if pp == 15 else None)
                    w2_consumed[gchunk] = t_mm2
                    if db + 2 < 8:
                        w2_g.append(issue_w2(q, e_, db + 2))

                    for g in range(2):
                        pyx = py0 if g == 0 else py1
                        sl = slice(db * 1024 + g * 512, db * 1024 + g * 512 + 512)
                        if e_ == 0:
                            def fa(eng, sl=sl, g=g, pyx=pyx):
                                return eng.tensor_mul(C[:, sl], pyx[:],
                                                      CEB[:, g * 512:(g + 1) * 512])
                            v_acc[db] = p.emit(
                                "vector", fa,
                                wait=[("t_pe", t_mm2), ("a_act", a_ceb)] if g == 0 else (),
                                inc=("v_dve", 1) if g == 1 else None)
                        else:
                            def fa1(eng, sl=sl, g=g, pyx=pyx):
                                return eng.tensor_mul(TSC[:, g * 512:(g + 1) * 512],
                                                      pyx[:],
                                                      CEB[:, g * 512:(g + 1) * 512])
                            v_a1 = p.emit("vector", fa1,
                                          wait=[("t_pe", t_mm2)] if g == 0 else (),
                                          inc=("v_dve", 1))
                            def fa2(eng, sl=sl, g=g):
                                return eng.tensor_add(C[:, sl], C[:, sl],
                                                      TSC[:, g * 512:(g + 1) * 512])
                            v_acc[db] = p.emit("vector", fa2, wait=[("v_dve", v_a1)],
                                               inc=("v_dve", 1) if g == 1 else None)
                    if e_ == ne - 1:
                        for g2 in range(2):
                            i2 = db * 2 + g2
                            sl2 = slice(db * 1024 + g2 * 512,
                                        db * 1024 + g2 * 512 + 512)
                            def fr(eng, sl2=sl2):
                                return eng.tensor_add(C[:, sl2], C[:, sl2],
                                                      XBF[:, sl2])
                            v_res2[i2] = p.emit("vector", fr, inc=("v_dve", 1))
                prev_e_v_acc = v_acc[7]
                if e_ == ne - 1:
                    last_ffn_pe = last_mm1_pe

            if dbg == "yf" and q == 0:
                p.emit("sync", lambda e: e.dma_start(out=DBG, in_=C[:]),
                    wait=[("v_dve", v_acc[7])], inc=("s_o", 16))
                out_dma_count += 1
                raise _Stop

            v_out = layer_norm(
                C, C, v_res2,
                (P[0][0:1, :], P[1][0:1, :]), (P[0][32:33, :], P[1][32:33, :]),
                (P[0], P[1]), (P[2], P[3]),
                pe_war_wait=[("v_dve", v_res2[15])])

            def fo(eng, b=b, s0=s0):
                return eng.dma_start(
                    out=OUT[b, :, s0:s0 + TOKQ].rearrange("(db p) s -> p db s", p=128),
                    in_=C[:].rearrange("p (db s) -> p db s", db=8))
            s_out_v = p.emit("sync", fo, wait=[("v_dve", v_out[15])],
                             inc=("s_o", 16))
            out_dma_count += 1

            if dbg is not None and q == dbg:
                def fd(eng):
                    return eng.dma_start(out=DBG, in_=C[:])
                p.emit("sync", fd, inc=("s_o", 16))
                out_dma_count += 1

            prev_ceb_a = a_ceb
            prev_vh15 = v_h[15]
            prev_vout15 = v_out[15]
            prev_vres2_15 = v_res2[15]
        except _Stop:
            pass

        p.emit("sync", lambda e: e.nop(), wait=[("s_o", 16 * out_dma_count)])

        with nc.Block() as block:
            @block.sync
            def _(eng):
                p.replay("sync", eng, sems)

            @block.gpsimd
            def _(eng):
                p.replay("gpsimd", eng, sems)

            @block.tensor
            def _(eng):
                p.replay("tensor", eng, sems)

            @block.scalar
            def _(eng):
                p.replay("scalar", eng, sems)

            @block.vector
            def _(eng):
                p.replay("vector", eng, sems)

    return nc


def _prep_inputs(L0_attn_x, L0_attn_W_lambda, L0_attn_b_lambda, L0_attn_W_B,
                 L0_attn_W_C, L0_attn_W_gate, L0_attn_rms_weight,
                 L0_ffn_router_w, L0_ffn_w13, L0_ffn_w2):
    x = np.asarray(L0_attn_x, dtype=np.float32)
    x_fm = np.ascontiguousarray(x.transpose(0, 2, 1))
    wl = np.asarray(L0_attn_W_lambda, np.float32)
    wb = np.asarray(L0_attn_W_B, np.float32)
    wlamb = np.ascontiguousarray(
        np.concatenate([wl.T, wb.T], axis=1))
    blam = np.ascontiguousarray(
        np.asarray(L0_attn_b_lambda, np.float32).reshape(NST, 1))
    wct = np.ascontiguousarray(np.asarray(L0_attn_W_C, np.float32).T)
    wgt = np.ascontiguousarray(np.asarray(L0_attn_W_gate, np.float32).T)
    rmsw = np.ascontiguousarray(
        np.asarray(L0_attn_rms_weight, np.float32).reshape(8, 128).T)
    rwt = np.ascontiguousarray(np.asarray(L0_ffn_router_w, np.float32).T)
    w13 = np.asarray(L0_ffn_w13, np.float32)
    arr = w13.reshape(NE, 8, 128, 32, 128).transpose(0, 3, 1, 2, 4)
    gate, up = arr[:, :16], arr[:, 16:]
    w13r = np.ascontiguousarray(
        np.stack([gate, up], axis=3)
        .reshape(NE, 256, 128, 128))
    w2 = np.asarray(L0_ffn_w2, np.float32)
    w2r = np.ascontiguousarray(
        w2.reshape(NE, 16, 128, 8, 128).transpose(0, 3, 1, 2, 4)
        .reshape(NE, 128, 128, 128))
    ohe = np.zeros((NE, 128 * NE), np.float32)
    for e in range(NE):
        ohe[e, e * 128:(e + 1) * 128] = 1.0
    ident = np.eye(128, dtype=np.float32)
    return x_fm, dict(
        wlamb=wlamb, blam=blam, wct=wct, wgt=wgt, rmsw=rmsw, rwt=rwt,
        w13r=w13r, w2r=w2r, ohe=ohe, ident=ident)


_NC_CACHE = {}


def kernel(**inputs):
    x_fm, shared = _prep_inputs(**inputs)
    key = "full"
    if key not in _NC_CACHE:
        _NC_CACHE[key] = build_nc()
    nc = _NC_CACHE[key]
    in_maps = []
    for core in range(8):
        m = dict(shared)
        m["x_fm"] = np.ascontiguousarray(x_fm[2 * core:2 * core + 2])
        in_maps.append(m)
    res = run_bass_kernel_spmd(nc, in_maps, core_ids=list(range(8)))
    outs = []
    for core in range(8):
        o = res.results[core]["out_fm"]
        outs.append(np.ascontiguousarray(o.transpose(0, 2, 1)))
    return np.concatenate(outs, axis=0).astype(np.float32)
